# version 02 (final)
# baseline (speedup 1.0000x reference)
import numpy as np
import concourse.bass as bass
import concourse.mybir as mybir
from concourse.bass_utils import run_bass_kernel_spmd

F32 = mybir.dt.float32
AF = mybir.ActivationFunctionType
ALU = mybir.AluOpType
AX = mybir.AxisListType

B, S, C = 4, 4096, 1024
HS = 2048
NCORE = 8


class Pr:
    def __init__(self):
        self.nc = bass.Bass("TRN2", target_bir_lowering=False)
        self.steps = []
        self.psb = [self.nc.alloc_psum_tensor("psb%d" % i, [128, 512], F32)
                    for i in range(8)]

    def inp(self, name, shape):
        return self.nc.dram_tensor(name, list(shape), F32, kind="ExternalInput").ap()

    def outp(self, name, shape):
        return self.nc.dram_tensor(name, list(shape), F32, kind="ExternalOutput").ap()

    def sb(self, name, shape):
        return self.nc.alloc_sbuf_tensor(name, list(shape), F32)

    def step(self, eng, fn, inc=1, wait=True):
        self.steps.append((eng, fn, inc, wait))

    def dma(self, dst, src, wait=True):
        self.step("sync", lambda e: e.dma_start(out=dst, in_=src), 16, wait)

    def mm(self, out, lhsT, rhs, start=True, stop=True):
        self.step("tensor", lambda e: e.matmul(out, lhsT, rhs, start=start, stop=stop))

    def trans(self, out, in_, ident):
        self.step("tensor", lambda e: e.transpose(out, in_, ident))

    def cp(self, out, in_):
        self.step("vector", lambda e: e.tensor_copy(out, in_))

    def tt(self, out, a, b, op):
        self.step("vector", lambda e: e.tensor_tensor(out, a, b, op))

    def ts(self, out, in0, s1, s2, op0, op1=ALU.bypass):
        self.step("vector", lambda e: e.tensor_scalar(out, in0, s1, s2, op0, op1))

    def tss(self, out, in0, s1, op):
        self.step("vector", lambda e: e.tensor_single_scalar(out, in0, s1, op))

    def stt(self, out, in0, scalar, in1, op0, op1):
        self.step("vector", lambda e: e.scalar_tensor_tensor(out, in0, scalar, in1, op0, op1))

    def red(self, out, in_, op, axis=AX.X):
        self.step("vector", lambda e: e.tensor_reduce(out, in_, axis, op))

    def recip(self, out, in_):
        self.step("vector", lambda e: e.reciprocal(out, in_))

    def act(self, out, in_, func, bias=0.0, scale=1.0):
        self.step("scalar", lambda e: e.activation(out, in_, func, bias=bias, scale=scale))

    def smul(self, out, in_, c):
        self.step("scalar", lambda e: e.mul(out, in_, c))

    def finalize(self):
        nc = self.nc
        gsem = nc.alloc_semaphore("g")
        cnt = 0
        sched = []
        for eng, fn, inc, wait in self.steps:
            sched.append((eng, fn, inc, cnt if wait else None))
            cnt += inc
        total = cnt
        with nc.Block() as blk:
            for en in ["sync", "vector", "scalar", "tensor", "gpsimd"]:
                mysteps = [s for s in sched if s[0] == en]
                if not mysteps and en != "sync":
                    continue

                def make(en, mysteps):
                    def emit(e):
                        for (_, fn, inc, wv) in mysteps:
                            if wv is not None:
                                e.wait_ge(gsem, wv)
                            fn(e).then_inc(gsem, inc)
                        if en == "sync":
                            e.wait_ge(gsem, total)
                    return emit
                getattr(blk, en)(make(en, mysteps))
        return nc


def _ln(p, xin, xn_out, scr, n=1024, eps=1e-5):
    m, cen, sq, vs, sd, inv = [t[:] for t in scr]
    p.red(m, xin, ALU.add)
    p.ts(m, m, 1.0 / n, None, ALU.mult)
    p.ts(cen, xin, m, None, ALU.subtract)
    p.act(sq, cen, AF.Square)
    p.red(vs, sq, ALU.add)
    p.ts(sd, vs, 1.0 / n, eps, ALU.mult, ALU.add)
    p.act(sd, sd, AF.Sqrt)
    p.recip(inv, sd)
    p.ts(xn_out, cen, inv, None, ALU.mult)


def build_l0a():
    p = Pr()
    XPREF = p.inp("XPREF", (S, C))
    XOWN = p.inp("XOWN", (HS, C))
    WK = p.inp("WK", (C, 128))
    WV = p.inp("WV", (C, 128))
    WQ = p.inp("WQ", (8, C, 128))
    WP = p.inp("WP", (C, C))
    BP = p.inp("BP", (1, C))
    QGROW = p.inp("QGROW", (1, HS))
    SCOL = p.inp("SCOL", (128, 32))
    IDENT = p.inp("IDENT", (128, 128))
    ONES = p.inp("ONES", (128, 128))
    APROJ = p.outp("APROJ", (HS, C))
    XNT = p.outp("XNT", (C, HS))

    ident = p.sb("ident", (128, 128))
    ones = p.sb("ones", (128, 128))
    wk = p.sb("wk", (128, 8 * 128))
    wv = p.sb("wv", (128, 8 * 128))
    wqh = p.sb("wqh", (128, 8 * 128))
    wp = p.sb("wp", (128, 8 * 1024))
    bp = p.sb("bp", (1, 1024))
    qgrow = p.sb("qgrow", (1, HS))
    scol = p.sb("scol", (128, 32))
    p.dma(ident[:], IDENT[:, :], wait=False)
    p.dma(ones[:], ONES[:, :], wait=False)
    p.dma(bp[:], BP[:, :], wait=False)
    p.dma(qgrow[:], QGROW[:, :], wait=False)
    p.dma(scol[:], SCOL[:, :], wait=False)
    for ci in range(8):
        p.dma(wk[:, ci * 128:(ci + 1) * 128], WK[ci * 128:(ci + 1) * 128, :], wait=False)
        p.dma(wv[:, ci * 128:(ci + 1) * 128], WV[ci * 128:(ci + 1) * 128, :], wait=False)
        p.dma(wp[:, ci * 1024:(ci + 1) * 1024], WP[ci * 128:(ci + 1) * 128, :], wait=False)

    xt = p.sb("xt", (128, 1024))
    xn = p.sb("xn", (128, 1024))
    xnt_tmp = p.sb("xnt_tmp", (128, 1024))
    scr = (p.sb("m", (128, 1)), xn, xnt_tmp,
           p.sb("vs", (128, 1)), p.sb("sd", (128, 1)), p.sb("inv", (128, 1)))
    KT = p.sb("KT", (128, S))
    V = p.sb("V", (128, 32 * 128))
    XNTo = p.sb("XNTo", (128, 8 * HS))

    ps = p.psb

    def prep_tile(src_ap, i, own):
        p.dma(xt[:], src_ap[i * 128:(i + 1) * 128, :])
        _ln(p, xt[:], xn[:], scr)
        for ci in range(8):
            p.trans(ps[0][:, 0:128], xn[:, ci * 128:(ci + 1) * 128], ident[:])
            p.cp(xnt_tmp[:, ci * 128:(ci + 1) * 128], ps[0][:, 0:128])
        if own:
            for ci in range(8):
                p.cp(XNTo[:, ci * HS + i * 128: ci * HS + (i + 1) * 128],
                     xnt_tmp[:, ci * 128:(ci + 1) * 128])
        else:
            for ci in range(8):
                p.mm(ps[1][:, 0:128], wk[:, ci * 128:(ci + 1) * 128],
                     xnt_tmp[:, ci * 128:(ci + 1) * 128], start=(ci == 0), stop=(ci == 7))
                p.mm(ps[2][:, 0:128], xnt_tmp[:, ci * 128:(ci + 1) * 128],
                     wv[:, ci * 128:(ci + 1) * 128], start=(ci == 0), stop=(ci == 7))
            p.cp(KT[:, i * 128:(i + 1) * 128], ps[1][:, 0:128])
            p.cp(V[:, i * 128:(i + 1) * 128], ps[2][:, 0:128])

    for i in range(32):
        prep_tile(XPREF, i, False)
    for j in range(16):
        prep_tile(XOWN, j, True)
    for ci in range(8):
        p.dma(XNT[ci * 128:(ci + 1) * 128, :], XNTo[:, ci * HS:(ci + 1) * HS])

    qT = p.sb("qT", (128, 512))
    eS = p.sb("eS", (128, 512))
    msk = p.sb("msk", (128, 512))
    qg = p.sb("qg", (128, 512))
    rsi = p.sb("rsi", (1, 512))
    bcr = p.sb("bcr", (128, 512))
    OUTT = p.sb("OUTT", (128, 8 * 512))
    pj = p.sb("pj", (128, 1024))

    for qb in range(4):
        p.mm(ps[3][:, 0:512], ones[0:1, :], qgrow[:, qb * 512:(qb + 1) * 512])
        p.cp(qg[:], ps[3][:, 0:512])
        for h in range(8):
            for ci in range(8):
                p.dma(wqh[:, ci * 128:(ci + 1) * 128],
                      WQ[h, ci * 128:(ci + 1) * 128, :])
            for ci in range(8):
                p.mm(ps[0][:, 0:512], wqh[:, ci * 128:(ci + 1) * 128],
                     XNTo[:, ci * HS + qb * 512: ci * HS + (qb + 1) * 512],
                     start=(ci == 0), stop=(ci == 7))
            p.cp(qT[:], ps[0][:, 0:512])
            p.ts(qT[:], qT[:], 1.0 / 32.0, None, ALU.mult)
            for si in range(32):
                p.mm(ps[1][:, 0:512], KT[:, si * 128:(si + 1) * 128], qT[:])
                p.act(eS[:], ps[1][:, 0:512], AF.Exp)
                p.tss(msk[:], qg[:], scol[:, si:si + 1], ALU.is_ge)
                p.tt(eS[:], eS[:], msk[:], ALU.mult)
                p.mm(ps[2][:, 0:512], V[:, si * 128:(si + 1) * 128], eS[:],
                     start=(si == 0), stop=(si == 31))
                p.mm(ps[4][0:1, 0:512], ones[:, 0:1], eS[:],
                     start=(si == 0), stop=(si == 31))
            p.recip(rsi[:], ps[4][0:1, 0:512])
            p.mm(ps[5][:, 0:512], ones[0:1, :], rsi[:])
            p.cp(bcr[:], ps[5][:, 0:512])
            p.tt(OUTT[:, h * 512:(h + 1) * 512], ps[2][:, 0:512], bcr[:], ALU.mult)
        for qt in range(4):
            for cc in range(2):
                for h in range(8):
                    p.mm(ps[6][:, 0:512], OUTT[:, h * 512 + qt * 128: h * 512 + (qt + 1) * 128],
                         wp[:, h * 1024 + cc * 512: h * 1024 + (cc + 1) * 512],
                         start=(h == 0), stop=False)
                p.mm(ps[6][:, 0:512], ones[0:1, 0:128], bp[:, cc * 512:(cc + 1) * 512],
                     start=False, stop=True)
                p.cp(pj[:, cc * 512:(cc + 1) * 512], ps[6][:, 0:512])
            p.dma(APROJ[qb * 512 + qt * 128: qb * 512 + (qt + 1) * 128, :], pj[:])
    return p


def build_l0b():
    p = Pr()
    XNT = p.inp("XNT", (C, HS))
    XOWN = p.inp("XOWN", (HS, C))
    APROJ = p.inp("APROJ", (HS, C))
    M1 = p.inp("M1", (C, 64))
    M2F = p.inp("M2F", (64, 4096))
    MBF = p.inp("MBF", (1, 4096))
    W1F = p.inp("W1F", (C, 4096))
    W2F = p.inp("W2F", (4096, C))
    B2 = p.inp("B2", (1, C))
    IDENT = p.inp("IDENT", (128, 128))
    ONES = p.inp("ONES", (128, 128))
    X1 = p.outp("X1", (HS, C))

    ident = p.sb("ident", (128, 128))
    ones = p.sb("ones", (128, 128))
    xnt = p.sb("xnt", (128, 8 * HS))
    m1 = p.sb("m1", (128, 8 * 64))
    m2c = p.sb("m2c", (64, 512))
    mbfc = p.sb("mbfc", (1, 512))
    b2 = p.sb("b2", (1, 1024))
    p.dma(ident[:], IDENT[:, :], wait=False)
    p.dma(ones[:], ONES[:, :], wait=False)
    p.dma(b2[:], B2[:, :], wait=False)
    for ci in range(8):
        p.dma(xnt[:, ci * HS:(ci + 1) * HS], XNT[ci * 128:(ci + 1) * 128, :], wait=False)
        p.dma(m1[:, ci * 64:(ci + 1) * 64], M1[ci * 128:(ci + 1) * 128, :], wait=False)

    ps = p.psb
    lrT = p.sb("lrT", (64, HS))
    for qc in range(4):
        for ci in range(8):
            p.mm(ps[0][0:64, 0:512], m1[:, ci * 64:(ci + 1) * 64],
                 xnt[:, ci * HS + qc * 512: ci * HS + (qc + 1) * 512],
                 start=(ci == 0), stop=(ci == 7))
        p.cp(lrT[:, qc * 512:(qc + 1) * 512], ps[0][0:64, 0:512])

    w1c = p.sb("w1c", (128, 8 * 512))
    w2c = p.sb("w2c", (128, 4 * 1024))
    gmax = p.sb("gmax", (128, 8))
    oh = p.sb("oh", (128, 512))
    msel = p.sb("msel", (128, 512))
    MID = p.sb("MID", (128, 8))
    rl = p.sb("rl", (128, 8))
    sS = p.sb("sS", (128, 512))
    sT = p.sb("sT", (128, 512))
    RES = p.sb("RES", (128, 8 * 1024))
    xt = p.sb("xt", (128, 1024))
    at = p.sb("at", (128, 1024))
    ot = p.sb("ot", (128, 1024))

    for th in range(2):
        for ch in range(8):
            for ci in range(8):
                p.dma(w1c[:, ci * 512:(ci + 1) * 512],
                      W1F[ci * 128:(ci + 1) * 128, ch * 512:(ch + 1) * 512])
            for k in range(4):
                p.dma(w2c[:, k * 1024:(k + 1) * 1024],
                      W2F[ch * 512 + k * 128: ch * 512 + (k + 1) * 128, :])
            p.dma(m2c[:], M2F[:, ch * 512:(ch + 1) * 512])
            p.dma(mbfc[:], MBF[:, ch * 512:(ch + 1) * 512])
            for tl in range(8):
                t = th * 8 + tl
                p.mm(ps[0][:, 0:512], lrT[:, t * 128:(t + 1) * 128],
                     m2c[:], start=True, stop=False)
                p.mm(ps[0][:, 0:512], ones[0:1, 0:128], mbfc[:],
                     start=False, stop=True)
                for ci in range(8):
                    p.mm(ps[1][:, 0:512], xnt[:, ci * HS + t * 128: ci * HS + (t + 1) * 128],
                         w1c[:, ci * 512:(ci + 1) * 512], start=(ci == 0), stop=(ci == 7))
                p.red(gmax[:].rearrange("p (g o) -> p g o", o=1),
                      ps[0][:, 0:512].rearrange("p (g k) -> p g k", k=64), ALU.max)
                for g in range(8):
                    p.tss(oh[:, g * 64:(g + 1) * 64], ps[0][:, g * 64:(g + 1) * 64],
                          gmax[:, g:g + 1], ALU.is_equal)
                p.tt(msel[:], ps[1][:, 0:512], oh[:], ALU.mult)
                p.red(MID[:].rearrange("p (g o) -> p g o", o=1),
                      msel[:].rearrange("p (g k) -> p g k", k=64), ALU.add)
                p.ts(rl[:], MID[:], 0.0, None, ALU.max)
                for g in range(8):
                    p.ts(sS[:, g * 64:(g + 1) * 64], oh[:, g * 64:(g + 1) * 64],
                         rl[:, g:g + 1], None, ALU.mult)
                for k in range(4):
                    p.trans(ps[2][:, k * 128:(k + 1) * 128], sS[:, k * 128:(k + 1) * 128],
                            ident[:])
                p.cp(sT[:], ps[2][:, 0:512])
                for cc in range(2):
                    for k in range(4):
                        p.mm(ps[3][:, 0:512], sT[:, k * 128:(k + 1) * 128],
                             w2c[:, k * 1024 + cc * 512: k * 1024 + (cc + 1) * 512],
                             start=(k == 0), stop=(k == 3))
                    if ch == 0:
                        p.cp(RES[:, tl * 1024 + cc * 512: tl * 1024 + (cc + 1) * 512],
                             ps[3][:, 0:512])
                    else:
                        p.tt(RES[:, tl * 1024 + cc * 512: tl * 1024 + (cc + 1) * 512],
                             RES[:, tl * 1024 + cc * 512: tl * 1024 + (cc + 1) * 512],
                             ps[3][:, 0:512], ALU.add)
        for tl in range(8):
            t = th * 8 + tl
            p.dma(xt[:], XOWN[t * 128:(t + 1) * 128, :])
            p.dma(at[:], APROJ[t * 128:(t + 1) * 128, :])
            for cc in range(2):
                p.mm(ps[4][:, 0:512], ones[0:1, 0:128], b2[:, cc * 512:(cc + 1) * 512])
                p.tt(ot[:, cc * 512:(cc + 1) * 512],
                     RES[:, tl * 1024 + cc * 512: tl * 1024 + (cc + 1) * 512],
                     ps[4][:, 0:512], ALU.add)
            p.tt(ot[:], ot[:], xt[:], ALU.add)
            p.tt(ot[:], ot[:], at[:], ALU.add)
            p.dma(X1[t * 128:(t + 1) * 128, :], ot[:])
    return p


def build_pre():
    p = Pr()
    NT = 17
    TW = NT * 128
    X1H = p.inp("X1H", (TW, C))
    QKVWT = p.inp("QKVWT", (C, 3072))
    CTAB = p.inp("CTAB", (C, 32))
    CBAB = p.inp("CBAB", (32, 1))
    CONVW = p.inp("CONVW", (C * 3, 4))
    IDENT = p.inp("IDENT", (128, 128))
    ONES = p.inp("ONES", (128, 128))
    BLK2 = p.inp("BLK2", (128, 2))
    PAT2 = p.inp("PAT2", (2, 128))
    QTO = p.outp("QTO", (16, 64, HS))
    KTO = p.outp("KTO", (16, 64, HS))
    VTO = p.outp("VTO", (16, 64, HS))
    ABO = p.outp("ABO", (32, HS))

    ident = p.sb("ident", (128, 128))
    ones = p.sb("ones", (128, 128))
    blk2 = p.sb("blk2", (128, 2))
    pat2 = p.sb("pat2", (2, 128))
    cbab = p.sb("cbab", (32, 1))
    p.dma(ident[:], IDENT[:, :], wait=False)
    p.dma(ones[:], ONES[:, :], wait=False)
    p.dma(blk2[:], BLK2[:, :], wait=False)
    p.dma(pat2[:], PAT2[:, :], wait=False)
    p.dma(cbab[:], CBAB[:, :], wait=False)

    xnt = p.sb("xnt", (128, 8 * TW))
    xt = p.sb("xt", (128, 1024))
    xn = p.sb("xn", (128, 1024))
    scr = (p.sb("m", (128, 1)), p.sb("cen", (128, 1024)), p.sb("sq", (128, 1024)),
           p.sb("vs", (128, 1)), p.sb("sd", (128, 1)), p.sb("inv", (128, 1)))
    ps = p.psb
    for i in range(NT):
        p.dma(xt[:], X1H[i * 128:(i + 1) * 128, :])
        _ln(p, xt[:], xn[:], scr)
        for ci in range(8):
            p.trans(ps[0][:, 0:128], xn[:, ci * 128:(ci + 1) * 128], ident[:])
            p.cp(xnt[:, ci * TW + i * 128: ci * TW + (i + 1) * 128], ps[0][:, 0:128])

    wt = p.sb("wt", (128, 8 * 128))
    cw = p.sb("cw", (128, 4))
    proj = p.sb("proj", (128, TW))
    cv = p.sb("cv", (128, HS + 3))
    sil = p.sb("sil", (128, HS))
    sq2 = p.sb("sq2", (128, HS))
    ssr = p.sb("ssr", (2, HS))
    invr = p.sb("invr", (2, HS))
    bc = p.sb("bc", (128, HS))
    NCH = 5

    for co in range(24):
        for ci in range(8):
            p.dma(wt[:, ci * 128:(ci + 1) * 128],
                  QKVWT[ci * 128:(ci + 1) * 128, co * 128:(co + 1) * 128])
        p.dma(cw[:], CONVW[co * 128:(co + 1) * 128, :])
        for ch in range(NCH):
            w = min(512, TW - ch * 512)
            for ci in range(8):
                p.mm(ps[1][:, 0:w], wt[:, ci * 128:(ci + 1) * 128],
                     xnt[:, ci * TW + ch * 512: ci * TW + ch * 512 + w],
                     start=(ci == 0), stop=(ci == 7))
            p.cp(proj[:, ch * 512: ch * 512 + w], ps[1][:, 0:w])
        p.ts(cv[:, 0:HS], proj[:, 125:125 + HS], cw[:, 0:1], None, ALU.mult)
        for j in range(1, 4):
            p.stt(cv[:, 0:HS], proj[:, 125 + j:125 + j + HS], cw[:, j:j + 1],
                  cv[:, 0:HS], ALU.mult, ALU.add)
        p.act(sil[:], cv[:, 0:HS], AF.Silu)
        if co < 16:
            p.act(sq2[:], sil[:], AF.Square)
            for ch in range(4):
                p.mm(ps[2][0:2, 0:512], blk2[:], sq2[:, ch * 512:(ch + 1) * 512])
                p.cp(ssr[:, ch * 512:(ch + 1) * 512], ps[2][0:2, 0:512])
            p.act(ssr[:], ssr[:], AF.Sqrt)
            p.ts(ssr[:], ssr[:], 1e-6, None, ALU.max)
            p.recip(invr[:], ssr[:])
            for ch in range(4):
                p.mm(ps[3][:, 0:512], pat2[:], invr[:, ch * 512:(ch + 1) * 512])
                p.cp(bc[:, ch * 512:(ch + 1) * 512], ps[3][:, 0:512])
            p.tt(sil[:], sil[:], bc[:], ALU.mult)
        qq = co % 8
        dst = [QTO, KTO, VTO][co // 8]
        p.dma(dst[2 * qq:2 * qq + 2, :, :].rearrange("a b t -> (a b) t"), sil[:])

    ctw = p.sb("ctw", (128, 8 * 32))
    for ci in range(8):
        p.dma(ctw[:, ci * 32:(ci + 1) * 32], CTAB[ci * 128:(ci + 1) * 128, :])
    abt = p.sb("abt", (32, HS))
    for ch in range(4):
        for ci in range(8):
            p.mm(ps[4][0:32, 0:512], ctw[:, ci * 32:(ci + 1) * 32],
                 xnt[:, ci * TW + 128 + ch * 512: ci * TW + 128 + (ch + 1) * 512],
                 start=(ci == 0), stop=(ci == 7))
        p.ts(abt[:, ch * 512:(ch + 1) * 512], ps[4][0:32, 0:512], cbab[:, 0:1], None, ALU.add)
    p.act(abt[:], abt[:], AF.Sigmoid)
    abf = p.sb("abf", (32, HS))
    p.ts(abf[:], abt[:], 0.98, 0.01, ALU.mult, ALU.add)
    p.dma(ABO[0:16, :], abf[0:16, :])
    p.dma(ABO[16:32, :], abt[16:32, :])
    return p


def build_scan():
    p = Pr()
    QTB = p.inp("QTB", (64, S))
    KTB = p.inp("KTB", (64, S))
    VTB = p.inp("VTB", (64, S))
    AB = p.inp("AB", (2, S))
    TRIUI = p.inp("TRIUI", (128, 128))
    MTRILN = p.inp("MTRILN", (128, 128))
    MTRIUN = p.inp("MTRIUN", (128, 128))
    IDENT = p.inp("IDENT", (128, 128))
    ONES = p.inp("ONES", (128, 128))
    NORMT = p.outp("NORMT", (64, S))

    triui = p.sb("triui", (128, 128))
    mtriln = p.sb("mtriln", (128, 128))
    mtriun = p.sb("mtriun", (128, 128))
    ident = p.sb("ident", (128, 128))
    ones = p.sb("ones", (128, 128))
    qT = p.sb("qT", (64, S))
    kT = p.sb("kT", (64, S))
    vT = p.sb("vT", (64, S))
    ab_a = p.sb("ab_a", (1, S))
    ab_b = p.sb("ab_b", (1, S))
    p.dma(triui[:], TRIUI[:, :], wait=False)
    p.dma(mtriln[:], MTRILN[:, :], wait=False)
    p.dma(mtriun[:], MTRIUN[:, :], wait=False)
    p.dma(ident[:], IDENT[:, :], wait=False)
    p.dma(ones[:], ONES[:, :], wait=False)
    p.dma(qT[:], QTB[:, :], wait=False)
    p.dma(kT[:], KTB[:, :], wait=False)
    p.dma(vT[:], VTB[:, :], wait=False)
    p.dma(ab_a[:], AB[0:1, :], wait=False)
    p.dma(ab_b[:], AB[1:2, :], wait=False)

    ps = p.psb
    acol = p.sb("acol", (128, 1))
    bcol = p.sb("bcol", (128, 1))
    lna = p.sb("lna", (128, 1))
    lac = p.sb("lac", (128, 1))
    lan = p.sb("lan", (128, 1))
    lar = p.sb("lar", (1, 128))
    d1 = p.sb("d1", (128, 128))
    G1 = p.sb("G1", (128, 128))
    G2 = p.sb("G2", (128, 128))
    GB = p.sb("GB", (64, 128))
    brow = p.sb("brow", (1, 128))
    t_a = p.sb("t_a", (128, 128))
    t_b = p.sb("t_b", (128, 128))
    Bm = p.sb("Bm", (128, 128))
    Bs = p.sb("Bs", (128, 128))
    Bm2 = p.sb("Bm2", (128, 128))
    Bs2 = p.sb("Bs2", (128, 128))
    Aqs = p.sb("Aqs", (128, 128))
    Ktok = p.sb("Ktok", (128, 64))
    Vtok = p.sb("Vtok", (128, 64))
    gcol = p.sb("gcol", (128, 1))
    gneg = p.sb("gneg", (128, 1))
    Ua = p.sb("Ua", (128, 64))
    Ub = p.sb("Ub", (128, 64))
    qg = p.sb("qg", (64, 128))
    ccol = p.sb("ccol", (128, 1))
    Uc = p.sb("Uc", (128, 64))
    Sta = p.sb("Sta", (64, 64))
    Stb = p.sb("Stb", (64, 64))
    SQ = p.sb("SQ", (128, 64))
    sm = p.sb("sm", (128, 1))
    sd = p.sb("sd", (128, 1))
    sinv = p.sb("sinv", (128, 1))
    nm = p.sb("nm", (128, 64))
    NO = p.sb("NO", (64, S))
    p.step("vector", lambda e: e.memset(Sta[:], 0.0))

    St = Sta
    Stn = Stb
    for c in range(32):
        cs = c * 128
        kc = kT[:, cs:cs + 128]
        qc = qT[:, cs:cs + 128]
        vc = vT[:, cs:cs + 128]
        p.trans(ps[0][:, 0:1], ab_a[0:1, cs:cs + 128], ident[0:1, 0:1])
        p.cp(acol[:], ps[0][:, 0:1])
        p.trans(ps[0][:, 1:2], ab_b[0:1, cs:cs + 128], ident[0:1, 0:1])
        p.cp(bcol[:], ps[0][:, 1:2])
        p.act(lna[:], acol[:], AF.Ln)
        p.mm(ps[1][:, 0:1], triui[:], lna[:])
        p.cp(lac[:], ps[1][:, 0:1])
        p.trans(ps[1][0:1, 0:128], lac[:], ident[:])
        p.cp(lar[:], ps[1][0:1, 0:128])
        p.mm(ps[2][:, 0:128], ones[0:1, :], lar[:])
        p.ts(d1[:], ps[2][:, 0:128], lac[:, 0:1], -1.0, ALU.subtract, ALU.mult)
        p.ts(d1[:], d1[:], 0.0, None, ALU.min)
        p.act(G1[:], d1[:], AF.Exp)
        p.ts(d1[:], ps[2][:, 0:128], lac[:, 0:1], 0.0, ALU.subtract, ALU.min)
        p.act(G2[:], d1[:], AF.Exp)
        p.act(GB[:], ps[2][0:64, 0:128], AF.Exp)
        p.act(gcol[:], lac[:], AF.Exp)
        p.ts(gneg[:], gcol[:], -1.0, None, ALU.mult)
        p.mm(ps[3][:, 0:128], kc, kc)
        p.mm(ps[4][:, 0:128], kc, qc)
        p.trans(ps[5][0:1, 0:128], bcol[:], ident[:])
        p.cp(brow[:], ps[5][0:1, 0:128])
        p.mm(ps[5][:, 0:128], ones[0:1, :], brow[:])
        p.tt(t_a[:], G1[:], ps[3][:, 0:128], ALU.mult)
        p.tt(t_a[:], t_a[:], ps[5][:, 0:128], ALU.mult)
        p.tt(Bm[:], t_a[:], mtriln[:], ALU.mult)
        p.tt(t_b[:], G2[:], ps[3][:, 0:128], ALU.mult)
        p.ts(t_b[:], t_b[:], bcol[:, 0:1], None, ALU.mult)
        p.tt(Bs[:], t_b[:], mtriun[:], ALU.mult)
        p.tt(t_b[:], G2[:], ps[4][:, 0:128], ALU.mult)
        p.ts(t_b[:], t_b[:], bcol[:, 0:1], None, ALU.mult)
        p.tt(Aqs[:], t_b[:], triui[:], ALU.mult)
        p.trans(ps[5][:, 128:192], kc, ident[0:64, 0:64])
        p.cp(Ktok[:], ps[5][:, 128:192])
        p.trans(ps[5][:, 192:256], vc, ident[0:64, 0:64])
        p.cp(Vtok[:], ps[5][:, 192:256])
        p.mm(ps[6][:, 0:64], kc, St[:])
        p.stt(Ua[:], ps[6][:, 0:64], gneg[:, 0:1], Vtok[:], ALU.mult, ALU.add)
        cur, nxt = Ua, Ub
        Bc, Bsc, Bn, Bsn = Bm, Bs, Bm2, Bs2
        for j in range(7):
            p.mm(ps[6][:, 64:128], Bsc[:], cur[:])
            p.tt(nxt[:], cur[:], ps[6][:, 64:128], ALU.add)
            cur, nxt = nxt, cur
            if j < 6:
                p.mm(ps[6][:, 128:256], Bsc[:], Bc[:])
                p.cp(Bn[:], ps[6][:, 128:256])
                p.mm(ps[6][:, 256:384], Bc[:], Bsc[:])
                p.cp(Bsn[:], ps[6][:, 256:384])
                Bc, Bn = Bn, Bc
                Bsc, Bsn = Bsn, Bsc
        U = cur
        p.tt(qg[:], qc, GB[:], ALU.mult)
        p.mm(ps[7][:, 0:64], qg[:], St[:], start=True, stop=False)
        p.mm(ps[7][:, 0:64], Aqs[:], U[:], start=False, stop=True)
        p.tt(ccol[:], G2[:, 127:128], bcol[:], ALU.mult)
        p.ts(Uc[:], U[:], ccol[:, 0:1], None, ALU.mult)
        p.mm(ps[6][0:64, 384:448], Ktok[:], Uc[:])
        p.stt(Stn[:], St[:], GB[:, 127:128], ps[6][0:64, 384:448], ALU.mult, ALU.add)
        St, Stn = Stn, St
        p.act(SQ[:], ps[7][:, 0:64], AF.Square)
        p.red(sm[:], SQ[:], ALU.add)
        p.ts(sd[:], sm[:], 1.0 / 64.0, 1e-6, ALU.mult, ALU.add)
        p.act(sd[:], sd[:], AF.Sqrt)
        p.recip(sinv[:], sd[:])
        p.ts(nm[:], ps[7][:, 0:64], sinv[:, 0:1], None, ALU.mult)
        p.trans(ps[7][0:64, 64:192], nm[:], ident[:])
        p.cp(NO[:, cs:cs + 128], ps[7][0:64, 64:192])
    p.dma(NORMT[:, :], NO[:])
    return p


def build_post():
    p = Pr()
    X1OWN = p.inp("X1OWN", (HS, C))
    NRM = p.inp("NRM", (16, 64, HS))
    OWT = p.inp("OWT", (C, C))
    NW2 = p.inp("NW2", (128, 1))
    CWG = p.inp("CWG", (C, C))
    CBG = p.inp("CBG", (1, C))
    OUTB = p.inp("OUTB", (1, C))
    IDENT = p.inp("IDENT", (128, 128))
    ONES = p.inp("ONES", (128, 128))
    ATTN1 = p.outp("ATTN1", (HS, C))
    XN1T = p.outp("XN1T", (C, HS))

    ident = p.sb("ident", (128, 128))
    ones = p.sb("ones", (128, 128))
    nw2 = p.sb("nw2", (128, 1))
    cbg = p.sb("cbg", (1, 1024))
    outb = p.sb("outb", (1, 1024))
    owt = p.sb("owt", (128, 8 * 1024))
    cwg = p.sb("cwg", (128, 8 * 1024))
    nrm = p.sb("nrm", (128, 8 * HS))
    p.dma(ident[:], IDENT[:, :], wait=False)
    p.dma(ones[:], ONES[:, :], wait=False)
    p.dma(nw2[:], NW2[:, :], wait=False)
    p.dma(cbg[:], CBG[:, :], wait=False)
    p.dma(outb[:], OUTB[:, :], wait=False)
    for ci in range(8):
        p.dma(owt[:, ci * 1024:(ci + 1) * 1024], OWT[ci * 128:(ci + 1) * 128, :], wait=False)
        p.dma(cwg[:, ci * 1024:(ci + 1) * 1024], CWG[ci * 128:(ci + 1) * 128, :], wait=False)
        p.dma(nrm[:, ci * HS:(ci + 1) * HS],
              NRM[2 * ci:2 * ci + 2, :, :].rearrange("a b t -> (a b) t"), wait=False)
    for ci in range(8):
        p.ts(owt[:, ci * 1024:(ci + 1) * 1024], owt[:, ci * 1024:(ci + 1) * 1024],
             nw2[:, 0:1], None, ALU.mult)

    ps = p.psb
    xt = p.sb("xt", (128, 1024))
    xn = p.sb("xn", (128, 1024))
    scr = (p.sb("m", (128, 1)), p.sb("cen", (128, 1024)), p.sb("sq", (128, 1024)),
           p.sb("vs", (128, 1)), p.sb("sd", (128, 1)), p.sb("inv", (128, 1)))
    xnt = p.sb("xnt", (128, 8 * 128))
    gs = p.sb("gs", (128, 512))
    a1 = p.sb("a1", (128, 1024))
    for t in range(16):
        p.dma(xt[:], X1OWN[t * 128:(t + 1) * 128, :])
        _ln(p, xt[:], xn[:], scr)
        for ci in range(8):
            p.trans(ps[0][:, 0:128], xn[:, ci * 128:(ci + 1) * 128], ident[:])
            p.cp(xnt[:, ci * 128:(ci + 1) * 128], ps[0][:, 0:128])
            p.dma(XN1T[ci * 128:(ci + 1) * 128, t * 128:(t + 1) * 128],
                  xnt[:, ci * 128:(ci + 1) * 128])
        for cc in range(2):
            for ci in range(8):
                p.mm(ps[1][:, 0:512], nrm[:, ci * HS + t * 128: ci * HS + (t + 1) * 128],
                     owt[:, ci * 1024 + cc * 512: ci * 1024 + (cc + 1) * 512],
                     start=(ci == 0), stop=False)
            p.mm(ps[1][:, 0:512], ones[0:1, 0:128], outb[:, cc * 512:(cc + 1) * 512],
                 start=False, stop=True)
            for ci in range(8):
                p.mm(ps[2][:, 0:512], xnt[:, ci * 128:(ci + 1) * 128],
                     cwg[:, ci * 1024 + cc * 512: ci * 1024 + (cc + 1) * 512],
                     start=(ci == 0), stop=False)
            p.mm(ps[2][:, 0:512], ones[0:1, 0:128], cbg[:, cc * 512:(cc + 1) * 512],
                 start=False, stop=True)
            p.act(gs[:], ps[2][:, 0:512], AF.Sigmoid)
            p.tt(a1[:, cc * 512:(cc + 1) * 512], ps[1][:, 0:512], gs[:], ALU.mult)
        p.dma(ATTN1[t * 128:(t + 1) * 128, :], a1[:])
    return p


def build_fa():
    p = Pr()
    RW = 769
    YTH = p.inp("YTH", (C, 4 * RW))
    WQKH = p.inp("WQKH", (C, 16 * 128))
    WV3 = p.inp("WV3", (C, C))
    QSC = p.inp("QSC", (128, 1))
    ATAB = p.inp("ATAB", (128, 4 * 768))
    BTAB = p.inp("BTAB", (128, 4 * 768))
    RPERM = p.inp("RPERM", (128, 128))
    WO = p.inp("WO", (C, C))
    QGROW = p.inp("QGROW", (1, HS))
    SCOLS = p.inp("SCOLS", (128, 24))
    SCOLP = p.inp("SCOLP", (128, 24))
    SVAL = p.inp("SVAL", (128, 24))
    IDENT = p.inp("IDENT", (128, 128))
    ONES = p.inp("ONES", (128, 128))
    YTO = p.outp("YTO", (C, HS))

    ident = p.sb("ident", (128, 128))
    ones = p.sb("ones", (128, 128))
    qsc = p.sb("qsc", (128, 1))
    rperm = p.sb("rperm", (128, 128))
    qgrow = p.sb("qgrow", (1, HS))
    scols = p.sb("scols", (128, 24))
    scolp = p.sb("scolp", (128, 24))
    sval = p.sb("sval", (128, 24))
    p.dma(ident[:], IDENT[:, :], wait=False)
    p.dma(ones[:], ONES[:, :], wait=False)
    p.dma(qsc[:], QSC[:, :], wait=False)
    p.dma(rperm[:], RPERM[:, :], wait=False)
    p.dma(qgrow[:], QGROW[:, :], wait=False)
    p.dma(scols[:], SCOLS[:, :], wait=False)
    p.dma(scolp[:], SCOLP[:, :], wait=False)
    p.dma(sval[:], SVAL[:, :], wait=False)

    ps = p.psb
    yth = p.sb("yth", (128, 8 * RW))
    wo = p.sb("wo", (128, 8 * 1024))
    for ci in range(8):
        p.dma(wo[:, ci * 1024:(ci + 1) * 1024], WO[ci * 128:(ci + 1) * 128, :], wait=False)
    wbuf = p.sb("wbuf", (128, 8 * 128))
    wvb = p.sb("wvb", (128, 8 * 1024))
    Vsb = p.sb("Vsb", (128, 6 * 1024))
    atab = p.sb("atab", (128, 768))
    btab = p.sb("btab", (128, 768))
    qg = p.sb("qg", (128, 512))
    mks = p.sb("mks", (128, 6 * 512))
    m1s = p.sb("m1s", (128, 512))
    qk = p.sb("qk", (128, 768))
    kh = p.sb("kh", (64, 768))
    sq2 = p.sb("sq2", (128, 768))
    ssr = p.sb("ssr", (2, 768))
    invr = p.sb("invr", (2, 768))
    bc2 = p.sb("bc2", (128, 768))
    rot = p.sb("rot", (128, 768))
    t1s = p.sb("t1s", (128, 768))
    eS = p.sb("eS", (128, 512))
    eS2 = p.sb("eS2", (128, 512))
    rsi = p.sb("rsi", (1, 512))
    bcr = p.sb("bcr", (64, 512))
    OUTT = p.sb("OUTT", (128, 8 * 512))
    osb = p.sb("osb", (128, 512))
    blk2 = p.sb("blk2", (128, 2))
    pat2 = p.sb("pat2", (2, 128))
    BLK2 = p.inp("BLK2", (128, 2))
    PAT2 = p.inp("PAT2", (2, 128))
    p.dma(blk2[:], BLK2[:, :], wait=False)
    p.dma(pat2[:], PAT2[:, :], wait=False)

    for qb in range(4):
        yb = qb * RW
        for ci in range(8):
            p.dma(yth[:, ci * RW:(ci + 1) * RW], YTH[ci * 128:(ci + 1) * 128, yb:yb + RW])
        p.dma(atab[:], ATAB[:, qb * 768:(qb + 1) * 768])
        p.dma(btab[:], BTAB[:, qb * 768:(qb + 1) * 768])
        p.mm(ps[0][:, 0:512], ones[0:1, :], qgrow[:, qb * 512:(qb + 1) * 512])
        p.cp(qg[:], ps[0][:, 0:512])
        for sg in range(6):
            i = qb * 6 + sg
            p.tss(m1s[:], qg[:], scols[:, i:i + 1], ALU.is_ge)
            p.tss(mks[:, sg * 512:(sg + 1) * 512], qg[:], scolp[:, i:i + 1], ALU.is_le)
            p.tt(mks[:, sg * 512:(sg + 1) * 512], mks[:, sg * 512:(sg + 1) * 512],
                 m1s[:], ALU.mult)
            p.ts(mks[:, sg * 512:(sg + 1) * 512], mks[:, sg * 512:(sg + 1) * 512],
                 sval[:, i:i + 1], None, ALU.mult)
            p.ts(mks[:, sg * 512:(sg + 1) * 512], mks[:, sg * 512:(sg + 1) * 512],
                 1e30, -1e30, ALU.mult, ALU.add)
        for ci in range(8):
            p.dma(wvb[:, ci * 1024:(ci + 1) * 1024],
                  WV3[ci * 128:(ci + 1) * 128, :])
        for t6 in range(6):
            for cc in range(2):
                for ci in range(8):
                    off = 1 + t6 * 128 - (1 if ci >= 4 else 0)
                    p.mm(ps[1][:, 0:512], yth[:, ci * RW + off: ci * RW + off + 128],
                         wvb[:, ci * 1024 + cc * 512: ci * 1024 + (cc + 1) * 512],
                         start=(ci == 0), stop=(ci == 7))
                p.cp(Vsb[:, t6 * 1024 + cc * 512: t6 * 1024 + (cc + 1) * 512], ps[1][:, 0:512])
        for h in range(16):
            for ci in range(8):
                p.dma(wbuf[:, ci * 128:(ci + 1) * 128],
                      WQKH[ci * 128:(ci + 1) * 128, h * 128:(h + 1) * 128])
            for c2 in range(2):
                for ci in range(8):
                    off = 1 + c2 * 384 - (1 if ci >= 4 else 0)
                    p.mm(ps[2][:, 0:384], wbuf[:, ci * 128:(ci + 1) * 128],
                         yth[:, ci * RW + off: ci * RW + off + 384],
                         start=(ci == 0), stop=(ci == 7))
                p.cp(qk[:, c2 * 384:(c2 + 1) * 384], ps[2][:, 0:384])
            p.act(sq2[:], qk[:], AF.Square)
            for c2 in range(2):
                p.mm(ps[3][0:2, 0:384], blk2[:], sq2[:, c2 * 384:(c2 + 1) * 384])
                p.cp(ssr[:, c2 * 384:(c2 + 1) * 384], ps[3][0:2, 0:384])
            p.act(ssr[:], ssr[:], AF.Sqrt)
            p.ts(ssr[:], ssr[:], 1e-12, None, ALU.max)
            p.recip(invr[:], ssr[:])
            for c2 in range(2):
                p.mm(ps[3][:, 0:384], pat2[:], invr[:, c2 * 384:(c2 + 1) * 384])
                p.cp(bc2[:, c2 * 384:(c2 + 1) * 384], ps[3][:, 0:384])
            p.tt(qk[:], qk[:], bc2[:], ALU.mult)
            p.ts(qk[:], qk[:], qsc[:, 0:1], None, ALU.mult)
            p.ts(qk[0:64, :], qk[0:64, :], 8.0, None, ALU.mult)
            for c2 in range(2):
                p.mm(ps[3][:, 0:384], rperm[:], qk[:, c2 * 384:(c2 + 1) * 384])
                p.cp(rot[:, c2 * 384:(c2 + 1) * 384], ps[3][:, 0:384])
            p.tt(t1s[:], atab[:], qk[:], ALU.mult)
            p.tt(rot[:], btab[:], rot[:], ALU.mult)
            p.tt(qk[:], t1s[:], rot[:], ALU.add)
            p.cp(kh[:], qk[64:128, :])
            for sg in range(6):
                p.mm(ps[4][:, 0:512], kh[:, sg * 128:(sg + 1) * 128],
                     qk[0:64, 256:768])
                p.tt(eS2[:], ps[4][:, 0:512], mks[:, sg * 512:(sg + 1) * 512], ALU.add)
                p.act(eS[:], eS2[:], AF.Exp)
                p.mm(ps[5][0:64, 0:512], Vsb[:, sg * 1024 + h * 64: sg * 1024 + h * 64 + 64],
                     eS[:], start=(sg == 0), stop=(sg == 5))
                p.mm(ps[6][0:1, 0:512], ones[:, 0:1], eS[:],
                     start=(sg == 0), stop=(sg == 5))
            p.recip(rsi[:], ps[6][0:1, 0:512])
            p.mm(ps[7][0:64, 0:512], ones[0:1, 0:64], rsi[:])
            p.cp(bcr[:], ps[7][0:64, 0:512])
            p.tt(OUTT[(h % 2) * 64:(h % 2) * 64 + 64, (h // 2) * 512:(h // 2 + 1) * 512],
                 ps[5][0:64, 0:512], bcr[:], ALU.mult)
        for ci in range(8):
            for dt in range(8):
                p.mm(ps[0][:, 0:512], wo[:, dt * 1024 + ci * 128: dt * 1024 + (ci + 1) * 128],
                     OUTT[:, dt * 512:(dt + 1) * 512], start=(dt == 0), stop=(dt == 7))
            p.tt(osb[:], ps[0][:, 0:512], yth[:, ci * RW + 257: ci * RW + 257 + 512], ALU.add)
            p.dma(YTO[ci * 128:(ci + 1) * 128, qb * 512:(qb + 1) * 512], osb[:])
    return p


def build_ff():
    p = Pr()
    TW = 2049
    YTH = p.inp("YTH", (C, TW))
    LNG = p.inp("LNG", (128, 8))
    LNB = p.inp("LNB", (128, 8))
    W1A = p.inp("W1A", (C, 2730))
    W1G = p.inp("W1G", (C, 2730))
    B1A = p.inp("B1A", (128, 22))
    B1G = p.inp("B1G", (128, 22))
    W2N = p.inp("W2N", (2730, C))
    B2C = p.inp("B2C", (128, 8))
    IDENT = p.inp("IDENT", (128, 128))
    ONES = p.inp("ONES", (128, 128))
    YTO = p.outp("YTO", (C, HS))

    ident = p.sb("ident", (128, 128))
    ones = p.sb("ones", (128, 128))
    lng = p.sb("lng", (128, 8))
    lnb = p.sb("lnb", (128, 8))
    b1a = p.sb("b1a", (128, 22))
    b1g = p.sb("b1g", (128, 22))
    b2c = p.sb("b2c", (128, 8))
    yth = p.sb("yth", (128, 8 * TW))
    p.dma(ident[:], IDENT[:, :], wait=False)
    p.dma(ones[:], ONES[:, :], wait=False)
    p.dma(lng[:], LNG[:, :], wait=False)
    p.dma(lnb[:], LNB[:, :], wait=False)
    p.dma(b1a[:], B1A[:, :], wait=False)
    p.dma(b1g[:], B1G[:, :], wait=False)
    p.dma(b2c[:], B2C[:, :], wait=False)
    for ci in range(8):
        p.dma(yth[:, ci * TW:(ci + 1) * TW], YTH[ci * 128:(ci + 1) * 128, :], wait=False)

    ps = p.psb
    mrow = p.sb("mrow", (1, 512))
    vrow = p.sb("vrow", (1, 512))
    m2r = p.sb("m2r", (1, 512))
    rsr = p.sb("rsr", (1, 512))
    sqt = p.sb("sqt", (128, 512))
    mbc = p.sb("mbc", (128, 512))
    rbc = p.sb("rbc", (128, 512))
    XN = p.sb("XN", (128, 8 * 512))
    wb1 = p.sb("wb1", (128, 8 * 128))
    wb1g = p.sb("wb1g", (128, 8 * 128))
    hg = p.sb("hg", (128, 512))
    Hsb = p.sb("Hsb", (128, 22 * 512))
    wb2 = p.sb("wb2", (128, 1024))
    osb = p.sb("osb", (128, 512))

    IT = [128] * 21 + [42]
    for ch in range(4):
        def shoff(ci):
            return (1 if ci < 4 else 0) + ch * 512
        for ci in range(8):
            p.mm(ps[0][0:1, 0:512], ones[:, 0:1],
                 yth[:, ci * TW + shoff(ci): ci * TW + shoff(ci) + 512],
                 start=(ci == 0), stop=(ci == 7))
        p.ts(mrow[:], ps[0][0:1, 0:512], 1.0 / 1024.0, None, ALU.mult)
        for ci in range(8):
            p.act(sqt[:], yth[:, ci * TW + shoff(ci): ci * TW + shoff(ci) + 512], AF.Square)
            p.mm(ps[1][0:1, 0:512], ones[:, 0:1], sqt[:], start=(ci == 0), stop=(ci == 7))
        p.ts(vrow[:], ps[1][0:1, 0:512], 1.0 / 1024.0, None, ALU.mult)
        p.tt(m2r[:], mrow[:], mrow[:], ALU.mult)
        p.tt(vrow[:], vrow[:], m2r[:], ALU.subtract)
        p.ts(vrow[:], vrow[:], 1e-5, None, ALU.add)
        p.act(vrow[:], vrow[:], AF.Sqrt)
        p.recip(rsr[:], vrow[:])
        p.mm(ps[2][:, 0:512], ones[0:1, :], mrow[:])
        p.cp(mbc[:], ps[2][:, 0:512])
        p.mm(ps[2][:, 0:512], ones[0:1, :], rsr[:])
        p.cp(rbc[:], ps[2][:, 0:512])
        for ci in range(8):
            p.tt(XN[:, ci * 512:(ci + 1) * 512],
                 yth[:, ci * TW + shoff(ci): ci * TW + shoff(ci) + 512], mbc[:], ALU.subtract)
            p.tt(XN[:, ci * 512:(ci + 1) * 512], XN[:, ci * 512:(ci + 1) * 512],
                 rbc[:], ALU.mult)
            p.ts(XN[:, ci * 512:(ci + 1) * 512], XN[:, ci * 512:(ci + 1) * 512],
                 lng[:, ci:ci + 1], lnb[:, ci:ci + 1], ALU.mult, ALU.add)
        for it in range(22):
            w = IT[it]
            for ci in range(8):
                p.dma(wb1[:, ci * 128: ci * 128 + w],
                      W1A[ci * 128:(ci + 1) * 128, it * 128: it * 128 + w])
                p.dma(wb1g[:, ci * 128: ci * 128 + w],
                      W1G[ci * 128:(ci + 1) * 128, it * 128: it * 128 + w])
            for ci in range(8):
                p.mm(ps[3][0:w, 0:512], wb1[:, ci * 128: ci * 128 + w],
                     XN[:, ci * 512:(ci + 1) * 512], start=(ci == 0), stop=(ci == 7))
            for ci in range(8):
                p.mm(ps[4][0:w, 0:512], wb1g[:, ci * 128: ci * 128 + w],
                     XN[:, ci * 512:(ci + 1) * 512], start=(ci == 0), stop=(ci == 7))
            p.ts(hg[0:w, :], ps[4][0:w, 0:512], b1g[0:w, it:it + 1], None, ALU.add)
            p.act(hg[0:w, :], hg[0:w, :], AF.Gelu)
            p.ts(Hsb[0:w, it * 512:(it + 1) * 512], ps[3][0:w, 0:512],
                 b1a[0:w, it:it + 1], None, ALU.add)
            p.tt(Hsb[0:w, it * 512:(it + 1) * 512], Hsb[0:w, it * 512:(it + 1) * 512],
                 hg[0:w, :], ALU.mult)
        for it in range(22):
            w = IT[it]
            p.dma(wb2[0:w, :], W2N[it * 128: it * 128 + w, :])
            for ci in range(8):
                p.mm(ps[ci][:, 0:512], wb2[0:w, ci * 128:(ci + 1) * 128],
                     Hsb[0:w, it * 512:(it + 1) * 512], start=(it == 0), stop=(it == 21))
        for ci in range(8):
            p.stt(osb[:], ps[ci][:, 0:512], b2c[:, ci:ci + 1],
                  yth[:, ci * TW + 1 + ch * 512: ci * TW + 1 + (ch + 1) * 512],
                  ALU.add, ALU.add)
            p.dma(YTO[ci * 128:(ci + 1) * 128, ch * 512:(ch + 1) * 512], osb[:])
    return p


def build_fin():
    p = Pr()
    Y6T = p.inp("Y6T", (C, HS))
    X1OWN = p.inp("X1OWN", (HS, C))
    A1OWN = p.inp("A1OWN", (HS, C))
    IDENT = p.inp("IDENT", (128, 128))
    OUT = p.outp("OUT", (HS, C))
    ident = p.sb("ident", (128, 128))
    p.dma(ident[:], IDENT[:, :], wait=False)
    y6 = p.sb("y6", (128, 8 * HS))
    for ci in range(8):
        p.dma(y6[:, ci * HS:(ci + 1) * HS], Y6T[ci * 128:(ci + 1) * 128, :], wait=False)
    ps = p.psb
    xt = p.sb("xt", (128, 1024))
    at = p.sb("at", (128, 1024))
    ot = p.sb("ot", (128, 1024))
    for t in range(16):
        p.dma(xt[:], X1OWN[t * 128:(t + 1) * 128, :])
        p.dma(at[:], A1OWN[t * 128:(t + 1) * 128, :])
        for ci in range(8):
            p.trans(ps[0][:, 0:128], y6[:, ci * HS + t * 128: ci * HS + (t + 1) * 128],
                    ident[:])
            p.tt(ot[:, ci * 128:(ci + 1) * 128], ps[0][:, 0:128],
                 xt[:, ci * 128:(ci + 1) * 128], ALU.add)
        p.tt(ot[:], ot[:], at[:], ALU.add)
        p.dma(OUT[t * 128:(t + 1) * 128, :], ot[:])
    return p


_PROGS = {}


def _prog(name):
    if name not in _PROGS:
        _PROGS[name] = {"l0a": build_l0a, "l0b": build_l0b, "pre": build_pre,
                        "scan": build_scan, "post": build_post, "fa": build_fa,
                        "ff": build_ff, "fin": build_fin}[name]().finalize()
    return _PROGS[name]


def _run(name, in_maps):
    nc = _prog(name)
    return run_bass_kernel_spmd(nc, in_maps, core_ids=list(range(NCORE)))


_DBG = None


def kernel(**inputs):
    f32 = np.float32
    g = {k: np.ascontiguousarray(np.asarray(v, dtype=f32)) for k, v in inputs.items()}
    x = g["L0_attn_x"]
    IDENT = np.eye(128, dtype=f32)
    ONES = np.ones((128, 128), dtype=f32)
    BLK2 = np.zeros((128, 2), dtype=f32)
    BLK2[0:64, 0] = 1.0
    BLK2[64:128, 1] = 1.0
    PAT2 = np.ascontiguousarray(BLK2.T)
    ar128 = np.arange(128, dtype=f32)
    cores = [(c // 2, (c % 2) * HS) for c in range(NCORE)]

    SCOL = (ar128[:, None] + 128.0 * np.arange(32, dtype=f32)[None, :]).astype(f32)
    WQ = g["L0_attn_Wq"]
    im = []
    for (b, hs) in cores:
        im.append({"XPREF": x[b], "XOWN": x[b, hs:hs + HS],
                   "WK": g["L0_attn_Wk"], "WV": g["L0_attn_Wv"], "WQ": WQ,
                   "WP": g["L0_attn_Wp"], "BP": g["L0_attn_bp"][None, :],
                   "QGROW": (hs + np.arange(HS, dtype=f32))[None, :],
                   "SCOL": SCOL, "IDENT": IDENT, "ONES": ONES})
    r = _run("l0a", im).results
    APROJ = [r[c]["APROJ"] for c in range(NCORE)]
    XNT = [r[c]["XNT"] for c in range(NCORE)]

    M2F = np.ascontiguousarray(g["L0_ffn_m2"].reshape(64, 4096))
    MBF = np.ascontiguousarray(g["L0_ffn_mb"].reshape(1, 4096))
    W1F = np.ascontiguousarray(g["L0_ffn_w1"].reshape(1024, 4096))
    W2F = np.ascontiguousarray(g["L0_ffn_w2"].transpose(1, 0, 2).reshape(4096, 1024))
    im = []
    for c, (b, hs) in enumerate(cores):
        im.append({"XNT": XNT[c], "XOWN": x[b, hs:hs + HS], "APROJ": APROJ[c],
                   "M1": g["L0_ffn_m1"], "M2F": M2F, "MBF": MBF, "W1F": W1F,
                   "W2F": W2F, "B2": g["L0_ffn_b2"][None, :],
                   "IDENT": IDENT, "ONES": ONES})
    r = _run("l0b", im).results
    X1 = np.zeros((B, S, C), dtype=f32)
    for c, (b, hs) in enumerate(cores):
        X1[b, hs:hs + HS] = r[c]["X1"]
    if _DBG is not None:
        _DBG["APROJ"] = APROJ
        _DBG["XNT"] = XNT
        _DBG["X1"] = X1.copy()

    QKVWT = np.ascontiguousarray(g["L1_attn_qkv_w"].T)
    CTAB = np.ascontiguousarray(g["L1_attn_ctrl_w"][0:32].T)
    CBAB = np.ascontiguousarray(g["L1_attn_ctrl_b"][0:32, None])
    CONVW = np.concatenate([g["L1_attn_convq_w"], g["L1_attn_convk_w"],
                            g["L1_attn_convv_w"]], axis=0)
    im = []
    for (b, hs) in cores:
        xh = np.zeros((2176, C), dtype=f32)
        lo = hs - 128
        s0 = max(0, lo)
        xh[s0 - lo:, :] = X1[b, s0:hs + HS]
        im.append({"X1H": xh, "QKVWT": QKVWT, "CTAB": CTAB, "CBAB": CBAB,
                   "CONVW": CONVW, "IDENT": IDENT, "ONES": ONES,
                   "BLK2": BLK2, "PAT2": PAT2})
    r = _run("pre", im).results
    QT = np.zeros((B, 16, 64, S), dtype=f32)
    KT = np.zeros((B, 16, 64, S), dtype=f32)
    VT = np.zeros((B, 16, 64, S), dtype=f32)
    ABT = np.zeros((B, 32, S), dtype=f32)
    for c, (b, hs) in enumerate(cores):
        QT[b, :, :, hs:hs + HS] = r[c]["QTO"]
        KT[b, :, :, hs:hs + HS] = r[c]["KTO"]
        VT[b, :, :, hs:hs + HS] = r[c]["VTO"]
        ABT[b, :, hs:hs + HS] = r[c]["ABO"]
    if _DBG is not None:
        _DBG["QT"] = QT.copy()
        _DBG["KT"] = KT.copy()
        _DBG["VT"] = VT.copy()
        _DBG["ABT"] = ABT.copy()

    one = np.ones((128, 128), dtype=f32)
    TRIUI = np.triu(one).astype(f32)
    MTRILN = (-np.tril(one, -1)).astype(f32)
    MTRIUN = (-np.triu(one, 1)).astype(f32)
    NORMT = np.zeros((B, 16, 64, S), dtype=f32)
    for j in range(8):
        im = []
        for c, (b, hs) in enumerate(cores):
            h = (c % 2) * 8 + j
            im.append({"QTB": QT[b, h], "KTB": KT[b, h], "VTB": VT[b, h],
                       "AB": np.ascontiguousarray(ABT[b, [h, 16 + h], :]),
                       "TRIUI": TRIUI, "MTRILN": MTRILN, "MTRIUN": MTRIUN,
                       "IDENT": IDENT, "ONES": ONES})
        r = _run("scan", im).results
        for c, (b, hs) in enumerate(cores):
            h = (c % 2) * 8 + j
            NORMT[b, h] = r[c]["NORMT"]

    OWT = np.ascontiguousarray(g["L1_attn_out_w"].T)
    NW2 = np.ascontiguousarray(np.concatenate([g["L1_attn_norm_w"],
                                               g["L1_attn_norm_w"]])[:, None])
    CWG = np.ascontiguousarray(g["L1_attn_ctrl_w"][32:].T)
    im = []
    for (b, hs) in cores:
        im.append({"X1OWN": X1[b, hs:hs + HS],
                   "NRM": np.ascontiguousarray(NORMT[b, :, :, hs:hs + HS]),
                   "OWT": OWT, "NW2": NW2, "CWG": CWG,
                   "CBG": g["L1_attn_ctrl_b"][None, 32:],
                   "OUTB": g["L1_attn_out_b"][None, :],
                   "IDENT": IDENT, "ONES": ONES})
    r = _run("post", im).results
    ATTN1 = np.zeros((B, S, C), dtype=f32)
    YT = np.zeros((B, C, S), dtype=f32)
    for c, (b, hs) in enumerate(cores):
        ATTN1[b, hs:hs + HS] = r[c]["ATTN1"]
        YT[b, :, hs:hs + HS] = r[c]["XN1T"]
    if _DBG is not None:
        _DBG["NORMT"] = NORMT.copy()
        _DBG["ATTN1"] = ATTN1.copy()
        _DBG["XN1T"] = YT.copy()

    RW = 769
    DH, WIN = 64, 256
    half = np.arange(0, DH, 2, dtype=np.float64)
    inv_freq = 1.0 / 10000.0 ** (half / DH)
    base_scale = (half + 0.4 * DH) / (1.4 * DH)
    RPERM = np.zeros((128, 128), dtype=f32)
    for base in (0, 64):
        for jj in range(32):
            RPERM[base + jj + 32, base + jj] = -1.0
            RPERM[base + jj, base + jj + 32] = 1.0
    Wqkv = g["L1_ffn_Wqkv"]
    Wo6 = g["L1_ffn_Wo"]
    W16 = g["L1_ffn_W1"]
    W26 = g["L1_ffn_W2"]
    b16 = g["L1_ffn_b1"]
    b26 = g["L1_ffn_b2"]
    lng6 = g["L1_ffn_ln_g"]
    lnb6 = g["L1_ffn_ln_b"]
    qs6 = g["L1_ffn_q_scale"]
    ks6 = g["L1_ffn_k_scale"]

    def tabs(hs):
        A = np.zeros((128, 4 * 768), dtype=f32)
        Bt = np.zeros((128, 4 * 768), dtype=f32)
        for qb in range(4):
            pos = hs + qb * 512 - 256 + np.arange(768)
            posc = np.clip(pos, 0, S - 1).astype(np.float64)
            fr = posc[:, None] * inv_freq[None, :]
            fr = np.concatenate([fr, fr], axis=1)
            power = (posc - S // 2) / (WIN // 2)
            sc = base_scale[None, :] ** power[:, None]
            sc = np.concatenate([sc, sc], axis=1)
            cq = (np.cos(fr) * sc).T
            ck = (np.cos(fr) / sc).T
            sq = (np.sin(fr) * sc).T
            sk = (np.sin(fr) / sc).T
            A[:, qb * 768:(qb + 1) * 768] = np.concatenate([cq, ck], axis=0)
            Bt[:, qb * 768:(qb + 1) * 768] = np.concatenate([sq, sk], axis=0)
        return A, Bt

    for i in range(6):
        WQKH = np.zeros((C, 16 * 128), dtype=f32)
        for h in range(16):
            WQKH[:, h * 128:h * 128 + 64] = Wqkv[i][:, h * 64:(h + 1) * 64]
            WQKH[:, h * 128 + 64:h * 128 + 128] = Wqkv[i][:, 1024 + h * 64:1024 + (h + 1) * 64]
        WV3 = np.ascontiguousarray(Wqkv[i][:, 2048:3072])
        QSC = np.concatenate([qs6[i], ks6[i]])[:, None].astype(f32)
        im = []
        for (b, hs) in cores:
            yh = np.zeros((C, 4 * RW), dtype=f32)
            for qb in range(4):
                lo = hs + qb * 512 - 257
                s0 = max(0, lo)
                yh[:, qb * RW + (s0 - lo): (qb + 1) * RW] = YT[b, :, s0:hs + qb * 512 + 512]
            A, Bt = tabs(hs)
            scols = (hs + np.arange(4, dtype=f32)[None, :].repeat(6, 1).reshape(1, 24) * 0)
            sc = np.zeros((128, 24), dtype=f32)
            for qb in range(4):
                for sg in range(6):
                    sc[:, qb * 6 + sg] = hs + qb * 512 - 256 + sg * 128 + ar128
            im.append({"YTH": yh, "WQKH": WQKH, "WV3": WV3, "QSC": QSC,
                       "ATAB": A, "BTAB": Bt, "RPERM": RPERM,
                       "WO": Wo6[i],
                       "QGROW": (hs + np.arange(HS, dtype=f32))[None, :],
                       "SCOLS": sc, "SCOLP": sc + 256.0,
                       "SVAL": (sc >= 0).astype(f32),
                       "IDENT": IDENT, "ONES": ONES, "BLK2": BLK2, "PAT2": PAT2})
        r = _run("fa", im).results
        for c, (b, hs) in enumerate(cores):
            YT[b, :, hs:hs + HS] = r[c]["YTO"]
        if _DBG is not None:
            _DBG["fa%d" % i] = YT.copy()

        LNG = np.ascontiguousarray(lng6[i].reshape(8, 128).T)
        LNB = np.ascontiguousarray(lnb6[i].reshape(8, 128).T)
        W1A = np.ascontiguousarray(W16[i][:, :2730])
        W1G = np.ascontiguousarray(W16[i][:, 2730:])
        B1A = np.zeros((128, 22), dtype=f32)
        B1G = np.zeros((128, 22), dtype=f32)
        for it in range(22):
            w = 128 if it < 21 else 42
            B1A[0:w, it] = b16[i][it * 128: it * 128 + w]
            B1G[0:w, it] = b16[i][2730 + it * 128: 2730 + it * 128 + w]
        B2C = np.ascontiguousarray(b26[i].reshape(8, 128).T)
        im = []
        for (b, hs) in cores:
            yh = np.zeros((C, 2049), dtype=f32)
            lo = hs - 1
            s0 = max(0, lo)
            yh[:, s0 - lo:] = YT[b, :, s0:hs + HS]
            im.append({"YTH": yh, "LNG": LNG, "LNB": LNB, "W1A": W1A, "W1G": W1G,
                       "B1A": B1A, "B1G": B1G, "W2N": W26[i], "B2C": B2C,
                       "IDENT": IDENT, "ONES": ONES})
        r = _run("ff", im).results
        for c, (b, hs) in enumerate(cores):
            YT[b, :, hs:hs + HS] = r[c]["YTO"]
        if _DBG is not None:
            _DBG["ff%d" % i] = YT.copy()

    im = []
    for (b, hs) in cores:
        im.append({"Y6T": np.ascontiguousarray(YT[b, :, hs:hs + HS]),
                   "X1OWN": X1[b, hs:hs + HS], "A1OWN": ATTN1[b, hs:hs + HS],
                   "IDENT": IDENT})
    r = _run("fin", im).results
    OUT = np.zeros((B, S, C), dtype=f32)
    for c, (b, hs) in enumerate(cores):
        OUT[b, hs:hs + HS] = r[c]["OUT"]
    return OUT
